# version 10
# baseline (speedup 1.0000x reference)
import math
from contextlib import ExitStack

import numpy as np
import ml_dtypes

import concourse.bass as bass
import concourse.mybir as mybir
from concourse.bass_utils import run_bass_kernel_spmd

F32 = mybir.dt.float32
BF16 = mybir.dt.bfloat16
I32 = mybir.dt.int32
AF = mybir.ActivationFunctionType
ALU = mybir.AluOpType

NCORES = 8
S = 8192
D = 4096
HEADS = 32
DH = 128
SCALE = 1.0 / math.sqrt(DH)
TOK = S // NCORES
HALO = 256
XT = TOK + 2 * HALO
NW = 5
II = 11008
NG = II // 128
EPS = 1e-5
NEG = -30000.0
BFDT = ml_dtypes.bfloat16


class Ctr:
    __slots__ = ("sem", "n")

    def __init__(self, sem):
        self.sem = sem
        self.n = 0


class Emitter:
    ENGINES = ("sync", "tensor", "vector", "scalar", "gpsimd")

    def __init__(self, nc):
        self.nc = nc
        self.q = {k: [] for k in self.ENGINES}

    def add(self, eng, fn):
        self.q[eng].append(fn)

    def flush(self):
        with self.nc.Block() as blk:
            for name in self.ENGINES:
                ops = self.q[name]
                if ops:
                    getattr(blk, name)(lambda eng, ops=tuple(ops):
                                       [f(eng) for f in ops])
        self.q = {k: [] for k in self.ENGINES}


class DmaLanes:

    def __init__(self, nc, E, name, n):
        self.lanes = [Ctr(E(nc.semaphore(f"{name}{i}"))) for i in range(n)]
        self.i = 0

    def issue(self, em, eng, make, waits=()):
        lane = self.lanes[self.i % len(self.lanes)]
        self.i += 1
        pre = lane.n

        def go(s, make=make, lane=lane, pre=pre, waits=tuple(waits)):
            for sem, val in waits:
                if val:
                    s.wait_ge(sem, val)
            if pre:
                s.wait_ge(lane.sem, pre)
            make(s).then_inc(lane.sem, 16)
        em.add(eng, go)
        lane.n += 16
        return (lane.sem, lane.n)

    def wait_all_fn(self):
        toks = [(l.sem, l.n) for l in self.lanes if l.n]

        def go(eng, toks=tuple(toks)):
            for sem, val in toks:
                eng.wait_ge(sem, val)
        return go


def build_program(upto="full"):
    nc = bass.Bass("TRN2", target_bir_lowering=False,
                   dynamic_dma_scratch_size=4096)

    xs = nc.dram_tensor("xs", [XT, D], F32, kind="ExternalInput").ap()
    wq = nc.dram_tensor("wq", [D, D], F32, kind="ExternalInput").ap()
    wk = nc.dram_tensor("wk", [DH, D], F32, kind="ExternalInput").ap()
    wv = nc.dram_tensor("wv", [DH, D], F32, kind="ExternalInput").ap()
    wo = nc.dram_tensor("wo", [D, D], F32, kind="ExternalInput").ap()
    gq = nc.dram_tensor("gq", [D, II], I32, kind="ExternalInput").ap()
    uq = nc.dram_tensor("uq", [D, II], I32, kind="ExternalInput").ap()
    dq = nc.dram_tensor("dq", [II, D], I32, kind="ExternalInput").ap()
    gs = nc.dram_tensor("gs", [32, II], F32, kind="ExternalInput").ap()
    us = nc.dram_tensor("us", [32, II], F32, kind="ExternalInput").ap()
    dsr = nc.dram_tensor("dsr", [32, 3 * D], F32, kind="ExternalInput").ap()
    c_id = nc.dram_tensor("c_id", [128, 128], BF16, kind="ExternalInput").ap()
    c_rt = nc.dram_tensor("c_rt", [128, 128], BF16, kind="ExternalInput").ap()
    c_e32 = nc.dram_tensor("c_e32", [128, 4096], BF16, kind="ExternalInput").ap()
    c_cos = nc.dram_tensor("c_cos", [128, XT], F32, kind="ExternalInput").ap()
    c_sin = nc.dram_tensor("c_sin", [128, XT], F32, kind="ExternalInput").ap()
    c_km = nc.dram_tensor("c_km", [128, 20], F32, kind="ExternalInput").ap()
    c_bl = nc.dram_tensor("c_bl", [128, 20], F32, kind="ExternalInput").ap()

    out = nc.dram_tensor("out", [TOK, D], F32, kind="ExternalOutput").ap()
    if upto == "attn":
        r2 = nc.dram_tensor("r2", [TOK, D], F32, kind="ExternalOutput").ap()
    else:
        r2 = nc.dram_tensor("r2", [TOK, D], F32).ap()

    em = Emitter(nc)

    with ExitStack() as top:
        E = top.enter_context
        t_id = E(nc.sbuf_tensor("t_id", [128, 128], BF16))
        t_rt = E(nc.sbuf_tensor("t_rt", [128, 128], BF16))
        t_e32 = E(nc.sbuf_tensor("t_e32", [128, 4096], BF16))
        t_cos = E(nc.sbuf_tensor("t_cos", [128, XT], F32))
        t_sin = E(nc.sbuf_tensor("t_sin", [128, XT], F32))
        t_km = E(nc.sbuf_tensor("t_km", [128, 20], F32))
        t_bl = E(nc.sbuf_tensor("t_bl", [128, 20], F32))

        ps_a = [E(nc.psum_tensor(f"ps_a{i}", [128, 512], F32)) for i in range(4)]
        ps_o = [E(nc.psum_tensor(f"ps_o{i}", [128, 512], F32)) for i in range(2)]
        ps_x = E(nc.psum_tensor("ps_x", [128, 2048], BF16))

        LW = DmaLanes(nc, E, "lw", 8)
        LO = DmaLanes(nc, E, "lo", 8)
        sPE = Ctr(E(nc.semaphore("sPE")))
        sVE = Ctr(E(nc.semaphore("sVE")))
        sSC = Ctr(E(nc.semaphore("sSC")))

        def pst(k):
            return ps_x[:, 1024 * k:1024 * (k + 1)]

        def psf(k):
            return ps_x[:, 1024 * k:1024 * (k + 1)].bitcast(F32)

        for dst, src in ((t_id, c_id), (t_rt, c_rt), (t_e32, c_e32),
                         (t_cos, c_cos), (t_sin, c_sin),
                         (t_km, c_km), (t_bl, c_bl)):
            LW.issue(em, "sync",
                     lambda s, dst=dst, src=src: s.dma_start(dst[:, :], src))
        em.add("sync", LW.wait_all_fn())
        em.flush()

        def ln_chunk(src_ap_fn, sts, mv, rstd, nmb, xlt, waits):
            def vstats(v, waits=tuple(waits), base=sVE.n):
                for sem, val in waits:
                    if val:
                        v.wait_ge(sem, val)
                src = src_ap_fn()
                for j in range(8):
                    ins = v.bn_stats(sts[:, 6 * j:6 * j + 6],
                                     src[:, 512 * j:512 * (j + 1)])
                    if j == 7:
                        ins.then_inc(sVE.sem, 1)
                v.wait_ge(sVE.sem, base + 1)
                v.bn_aggr(mv[:, :], sts[:, :]).then_inc(sVE.sem, 1)
                v.wait_ge(sVE.sem, base + 2)
                v.tensor_scalar(rstd[:, :], mv[:, 1:2], EPS, None,
                                op0=ALU.add).then_inc(sVE.sem, 1)
            em.add("vector", vstats)
            sVE.n += 3
            v_stats = sVE.n

            def ssqrt(s, w=v_stats):
                s.wait_ge(sVE.sem, w)
                s.activation(rstd[:, :], rstd[:, :], AF.Sqrt
                             ).then_inc(sSC.sem, 1)
            em.add("scalar", ssqrt)
            sSC.n += 1
            s_sqrt = sSC.n

            def vbias(v, w=s_sqrt, base=sVE.n):
                v.wait_ge(sSC.sem, w)
                v.reciprocal(rstd[:, :], rstd[:, :]).then_inc(sVE.sem, 1)
                v.wait_ge(sVE.sem, base + 1)
                v.tensor_scalar(nmb[:, :], mv[:, 0:1], rstd[:, 0:1], -1.0,
                                op0=ALU.mult, op1=ALU.mult
                                ).then_inc(sVE.sem, 1)
            em.add("vector", vbias)
            sVE.n += 2
            v_bias = sVE.n

            def sapply(s, w=v_bias):
                s.wait_ge(sVE.sem, w)
                src = src_ap_fn()
                s.activation(xlt[:, :], src, AF.Identity, bias=nmb[:, :],
                             scale=rstd[:, :]).then_inc(sSC.sem, 1)
            em.add("scalar", sapply)
            sSC.n += 1
            return v_bias, sSC.n

        pst_free = [[], []]

        def tr8(src_sb, src_cols, dst_ap, extra_pe_wait=0):
            half = tr8.rr
            tr8.rr ^= 1
            guard = tuple(pst_free[half])

            def ptr(t, half=half, guard=guard, w2=extra_pe_wait,
                    cols=tuple(src_cols)):
                for gsem, gval in guard:
                    t.wait_ge(gsem, gval)
                if w2:
                    t.wait_ge(sSC.sem, w2)
                for j, c in enumerate(cols):
                    ins = t.transpose(pst(half)[:, 128 * j:128 * (j + 1)],
                                      src_sb[:, c:c + 128], t_id[:, :])
                    if j == len(cols) - 1:
                        ins.then_inc(sPE.sem, 1)
            em.add("tensor", ptr)
            sPE.n += 1
            p_done = sPE.n

            def sev(s, half=half, w=p_done, dst_ap=dst_ap, n=len(src_cols)):
                s.wait_ge(sPE.sem, w)
                s.copy(dst_ap, pst(half)[:, 0:128 * n]).then_inc(sSC.sem, 1)
            em.add("scalar", sev)
            sSC.n += 1
            pst_free[half] = [(sSC.sem, sSC.n)]
            return sSC.n
        tr8.rr = 0

        xln_scope = ExitStack()
        xln = xln_scope.enter_context(
            nc.sbuf_tensor("xln", [128, 32 * TOK], BF16))
        kv_scope = ExitStack()
        kr = kv_scope.enter_context(nc.sbuf_tensor("kr", [128, XT], BF16))
        vaug = kv_scope.enter_context(
            nc.sbuf_tensor("vaug", [128, 12 * 130], BF16))
        xlh_scope = ExitStack()
        xlh = xlh_scope.enter_context(
            nc.sbuf_tensor("xlh", [128, 32 * 512], BF16))

        with ExitStack() as ph:
            e = ph.enter_context
            xch = e(nc.sbuf_tensor("xch", [128, 2, D], F32))
            sts = e(nc.sbuf_tensor("sts", [128, 48], F32))
            mv = e(nc.sbuf_tensor("mv", [128, 2], F32))
            rstd = e(nc.sbuf_tensor("rstd", [128, 1], F32))
            nmb = e(nc.sbuf_tensor("nmb", [128, 1], F32))
            xlt = e(nc.sbuf_tensor("xlt", [128, D], BF16))

            lnfree = [(0, 0)] * 12
            for tl in range(12):
                dwaits = []
                if tl >= 2:
                    dwaits += [(sVE.sem, lnfree[tl - 2][0]),
                               (sSC.sem, lnfree[tl - 2][1])]
                tok = LW.issue(em, "sync",
                               lambda s, tl=tl: s.dma_start(
                                   xch[:, tl % 2, :],
                                   xs[tl * 128:(tl + 1) * 128, :]),
                               waits=dwaits)

                waits = [tok]
                if tl >= 1:
                    waits.append((sSC.sem, lnfree[tl - 1][1]))
                lnfree[tl] = ln_chunk(
                    (lambda tl=tl: xch[:, tl % 2, :]),
                    sts, mv, rstd, nmb, xlt, waits)

                for b in range(4):
                    d0 = 8 * b
                    if 2 <= tl <= 9:
                        ap = bass.AP(xln, TOK * d0 + (tl - 2) * 128,
                                     [[32 * TOK, 128], [TOK, 8], [1, 128]])
                    else:
                        hc = {0: 0, 1: 1, 10: 2, 11: 3}[tl]
                        ap = bass.AP(xlh, 512 * d0 + hc * 128,
                                     [[32 * 512, 128], [512, 8], [1, 128]])
                    tr8(xlt, [128 * (d0 + j) for j in range(8)], ap,
                        extra_pe_wait=lnfree[tl][1])
            em.flush()

        with ExitStack() as ph:
            e = ph.enter_context
            kvw = e(nc.sbuf_tensor("kvw", [128, D], F32))
            kvb = e(nc.sbuf_tensor("kvb", [128, D], BF16))
            wt_kv = e(nc.sbuf_tensor("wt_kv", [128, 2 * D], BF16))
            ksb = e(nc.sbuf_tensor("ksb", [128, XT], BF16))
            vsb = e(nc.sbuf_tensor("vsb", [128, XT], BF16))
            rtm = e(nc.sbuf_tensor("rtm", [128, 2, 512], F32))

            regions = [
                ("xlh", lambda d: 512 * d + 0, 256, 2, 0),
                ("xln", lambda d: TOK * d + 0, 512, 0, 0),
                ("xln", lambda d: TOK * d + 512, 512, 1, 0),
                ("xlh", lambda d: 512 * d + 256, 256, 2, 256),
            ]
            stor = {"xlh": xlh, "xln": xln}

            sc_kv = [0, 0]
            v_c_prev = 0
            for i, w_ap in enumerate((wk, wv)):
                dwaits = [(sVE.sem, v_c_prev)] if i else []
                tok = LW.issue(em, "sync",
                               lambda s, w_ap=w_ap:
                               s.dma_start(kvw[:, :], w_ap), waits=dwaits)

                def vcast(v, w=tok, wf=(sc_kv[0] if i else 0)):
                    v.wait_ge(*w)
                    if wf:
                        v.wait_ge(sSC.sem, wf)
                    v.tensor_copy(kvb[:, :], kvw[:, :]).then_inc(sVE.sem, 1)
                em.add("vector", vcast)
                sVE.n += 1
                v_c = sVE.n
                v_c_prev = v_c

                em.add("tensor", lambda t, w=v_c: t.wait_ge(sVE.sem, w))
                for b in range(4):
                    d0 = 8 * b
                    tr8(kvb, [128 * (d0 + j) for j in range(8)],
                        wt_kv[:, D * i + 1024 * b:D * i + 1024 * (b + 1)])
                sc_wt = sSC.n

                def pmm(t, i=i, w=sc_wt):
                    t.wait_ge(sSC.sem, w)
                    nreg = len(regions)
                    for ri, (st, base, width, ia, dc) in enumerate(regions):
                        tens = stor[st]
                        dst = ps_a[ia]
                        for d in range(32):
                            ins = t.matmul(dst[:, dc:dc + width],
                                           wt_kv[:, D * i + 128 * d:
                                                 D * i + 128 * (d + 1)],
                                           tens[:, base(d):base(d) + width],
                                           start=(d == 0), stop=(d == 31))
                            if ri == nreg - 1 and d == 31:
                                ins.then_inc(sPE.sem, 1)
                em.add("tensor", pmm)
                sPE.n += 1
                p_kv = sPE.n

                tgt = ksb if i == 0 else vsb

                def sev(s, tgt=tgt, w=p_kv):
                    s.wait_ge(sPE.sem, w)
                    s.copy(tgt[:, 0:256], ps_a[2][:, 0:256])
                    s.copy(tgt[:, 256:768], ps_a[0][:, :])
                    s.copy(tgt[:, 768:1280], ps_a[1][:, :])
                    s.copy(tgt[:, 1280:1536], ps_a[2][:, 256:512]
                           ).then_inc(sSC.sem, 1)
                em.add("scalar", sev)
                sSC.n += 1
                sc_kv[i] = sSC.n

            v_kr = 0
            for j in range(3):
                half = j % 2
                guard = tuple(pst_free[half])

                def prot(t, j=j, half=half, guard=guard,
                         w=sc_kv[0]):
                    t.wait_ge(sSC.sem, w)
                    for gsem, gval in guard:
                        t.wait_ge(gsem, gval)
                    t.matmul(psf(half), t_rt[:, :],
                             ksb[:, 512 * j:512 * (j + 1)],
                             start=True, stop=True).then_inc(sPE.sem, 1)
                em.add("tensor", prot)
                sPE.n += 1
                p_r = sPE.n

                def vkr(v, j=j, half=half, w=p_r, base=sVE.n, wprev=v_kr):
                    v.wait_ge(sPE.sem, w)
                    if wprev:
                        v.wait_ge(sVE.sem, wprev)
                    sl = slice(512 * j, 512 * (j + 1))
                    v.tensor_tensor(rtm[:, 0, :], ksb[:, sl], t_cos[:, sl],
                                    op=ALU.mult)
                    v.tensor_tensor(rtm[:, 1, :], psf(half), t_sin[:, sl],
                                    op=ALU.mult).then_inc(sVE.sem, 1)
                    v.wait_ge(sVE.sem, base + 1)
                    v.tensor_tensor(kr[:, sl], rtm[:, 0, :], rtm[:, 1, :],
                                    op=ALU.add).then_inc(sVE.sem, 1)
                em.add("vector", vkr)
                sVE.n += 2
                v_kr = sVE.n
                pst_free[half] = [(sVE.sem, v_kr)]

            for b in range(2):
                half = b % 2
                guard = tuple(pst_free[half])

                def ptv(t, b=b, half=half, guard=guard,
                        w=sc_kv[1]):
                    t.wait_ge(sSC.sem, w)
                    for gsem, gval in guard:
                        t.wait_ge(gsem, gval)
                    for j6 in range(6):
                        j = 6 * b + j6
                        ins = t.transpose(
                            pst(half)[:, 130 * j6:130 * j6 + 128],
                            vsb[:, 128 * j:128 * (j + 1)], t_id[:, :])
                        if j6 == 5:
                            ins.then_inc(sPE.sem, 1)
                em.add("tensor", ptv)
                sPE.n += 1
                p_v = sPE.n

                def sva(s, b=b, half=half, w=p_v):
                    s.wait_ge(sPE.sem, w)
                    ap_out = bass.AP(vaug, 130 * 6 * b,
                                     [[12 * 130, 128], [130, 6], [1, 128]])
                    ap_in = bass.AP(ps_x, 1024 * half,
                                    [[2048, 128], [130, 6], [1, 128]])
                    s.copy(ap_out, ap_in).then_inc(sSC.sem, 1)
                em.add("scalar", sva)
                sSC.n += 1
                pst_free[half] = [(sSC.sem, sSC.n)]

            def vones(v, w=sSC.n):
                v.wait_ge(sSC.sem, w)
                ap = bass.AP(vaug, 128, [[12 * 130, 128], [130, 12], [1, 1]])
                v.memset(ap, 1.0).then_inc(sVE.sem, 1)
            em.add("vector", vones)
            sVE.n += 1
            em.flush()
            pst_free = [[], []]

        xlh_scope.close()
        att_scope = ExitStack()
        att = att_scope.enter_context(
            nc.sbuf_tensor("att", [128, 8 * D], BF16, side="right"))

        def win_geom(w):
            q0 = max(0, 256 * (w - 1))
            q1 = min(TOK, 256 * (w - 1) + 512)
            return q0, q1 - q0, q0 - 256 * (w - 1)

        with ExitStack() as ph:
            e = ph.enter_context
            wqr = e(nc.sbuf_tensor("wqr", [128, 2, 2048], F32))
            wqb = e(nc.sbuf_tensor("wqb", [128, D], BF16))
            wqt = e(nc.sbuf_tensor("wqt", [128, D], BF16))
            qsb = e(nc.sbuf_tensor("qsb", [128, TOK], BF16))
            qr_ = e(nc.sbuf_tensor("qr_", [128, TOK], BF16))
            sgm = e(nc.sbuf_tensor("sgm", [128, 2, 2048], BF16))
            onr = e(nc.sbuf_tensor("onr", [128, 2, 512], BF16))
            den = e(nc.sbuf_tensor("den", [128, 2, 4], F32))
            rbw = e(nc.sbuf_tensor("rbw", [128, 2, 4], F32))
            rtm2 = e(nc.sbuf_tensor("rtm2", [128, 2, 512], F32))

            v_cast = 0
            sc_wqt = 0
            p_qmm = 0
            v_qr = 0
            prev = {"sg": 0, "om": 0, "den": 0, "on": 0, "acc": 0, "sc": 0}
            prev2 = dict(prev)

            def wqprep(h):
                nonlocal v_cast
                toks = []
                for half in range(2):
                    dwaits = [(sVE.sem, v_cast)] if (h >= 1 and half == 0) \
                        else []
                    toks.append(LW.issue(
                        em, "sync",
                        lambda s, h=h, half=half: s.dma_start(
                            wqr[:, half, :],
                            wq[128 * h:128 * (h + 1),
                               2048 * half:2048 * (half + 1)]),
                        waits=dwaits))

                def vcast(v, w0=toks[0], w1=toks[1], wf=sc_wqt):
                    v.wait_ge(*w0)
                    v.wait_ge(*w1)
                    if wf:
                        v.wait_ge(sSC.sem, wf)
                    v.tensor_copy(wqb[:, 0:2048], wqr[:, 0, :])
                    v.tensor_copy(wqb[:, 2048:4096], wqr[:, 1, :]
                                  ).then_inc(sVE.sem, 1)
                em.add("vector", vcast)
                sVE.n += 1
                v_cast = sVE.n

            def wqtr(b):
                if b == 0:
                    em.add("tensor",
                           lambda t, w=v_cast: t.wait_ge(sVE.sem, w))
                d0 = 8 * b
                tr8(wqb, [128 * (d0 + j) for j in range(8)],
                    wqt[:, 1024 * b:1024 * (b + 1)])

            wqprep(0)
            for b in range(4):
                wqtr(b)
            sc_wqt = sSC.n

            for h in range(HEADS):
                if h + 1 < HEADS:
                    wqprep(h + 1)

                def qmm(t, w=sc_wqt, wfree=prev["sg"]):
                    t.wait_ge(sSC.sem, w)
                    if wfree:
                        t.wait_ge(sSC.sem, wfree)
                    for th in range(2):
                        for d in range(32):
                            ins = t.matmul(ps_a[th][:, :],
                                           wqt[:, 128 * d:
                                               128 * (d + 1)],
                                           xln[:, TOK * d + 512 * th:
                                               TOK * d + 512 * (th + 1)],
                                           start=(d == 0), stop=(d == 31))
                            if d == 31:
                                ins.then_inc(sPE.sem, 1)
                em.add("tensor", qmm)
                sPE.n += 2
                p_qmm = sPE.n

                def qev(s, w=p_qmm, wf=v_qr):
                    s.wait_ge(sPE.sem, w - 1)
                    if wf:
                        s.wait_ge(sVE.sem, wf)
                    s.copy(qsb[:, 0:512], ps_a[0][:, :])
                    s.wait_ge(sPE.sem, w)
                    s.copy(qsb[:, 512:1024], ps_a[1][:, :]
                           ).then_inc(sSC.sem, 1)
                em.add("scalar", qev)
                sSC.n += 1
                sc_qev = sSC.n

                def qrot(t, w=sc_qev, wfree=prev["sg"]):
                    t.wait_ge(sSC.sem, w)
                    if wfree:
                        t.wait_ge(sSC.sem, wfree)
                    for th in range(2):
                        ins = t.matmul(ps_a[2 + th][:, :], t_rt[:, :],
                                       qsb[:, 512 * th:512 * (th + 1)],
                                       start=True, stop=True)
                        if th == 1:
                            ins.then_inc(sPE.sem, 1)
                em.add("tensor", qrot)
                sPE.n += 1
                p_rot = sPE.n

                def vqr(v, w=p_rot, wfree=prev["sc"], base=sVE.n,
                        wprev=v_qr):
                    v.wait_ge(sPE.sem, w)
                    if wfree:
                        v.wait_ge(sPE.sem, wfree)
                    if wprev:
                        v.wait_ge(sVE.sem, wprev)
                    for th in range(2):
                        sl = slice(HALO + 512 * th, HALO + 512 * (th + 1))
                        qs = slice(512 * th, 512 * (th + 1))
                        if th == 1:
                            v.wait_ge(sVE.sem, base + 2)
                        v.tensor_tensor(rtm2[:, 0, :], qsb[:, qs],
                                        t_cos[:, sl], op=ALU.mult)
                        v.tensor_tensor(rtm2[:, 1, :], ps_a[2 + th][:, :],
                                        t_sin[:, sl], op=ALU.mult
                                        ).then_inc(sVE.sem, 1)
                        v.wait_ge(sVE.sem, base + 1 + 2 * th)
                        v.tensor_tensor(qr_[:, qs], rtm2[:, 0, :],
                                        rtm2[:, 1, :], op=ALU.add
                                        ).then_inc(sVE.sem, 1)
                em.add("vector", vqr)
                sVE.n += 4
                v_qr = sVE.n

                for w in range(NW):
                    q0, qe, qw0 = win_geom(w)
                    nqc = qe // 128
                    sl2 = w % 2

                    def psc(t, w=w, q0=q0, qe=qe, wqr_=v_qr,
                            wfree=prev.get("sgb", (0, 0, 0, 0))):
                        t.wait_ge(sVE.sem, wqr_)
                        for j in range(4):
                            if wfree[j]:
                                t.wait_ge(sSC.sem, wfree[j])
                            ins = t.matmul(ps_a[j][:, 0:qe],
                                           kr[:, 256 * w + 128 * j:
                                              256 * w + 128 * (j + 1)],
                                           qr_[:, q0:q0 + qe],
                                           start=True, stop=True)
                            if j == 3:
                                ins.then_inc(sPE.sem, 1)
                    em.add("tensor", psc)
                    sPE.n += 1
                    cur_sc = sPE.n

                    def ssg(s, w=w, qe=qe, sl2=sl2, wp=cur_sc,
                            wfree=prev2["om"]):
                        s.wait_ge(sPE.sem, wp)
                        if wfree:
                            s.wait_ge(sPE.sem, wfree)
                        for j in range(4):
                            s.activation(
                                sgm[:, sl2, 512 * j:512 * j + qe],
                                ps_a[j][:, 0:qe], AF.Sigmoid,
                                bias=t_km[:, 4 * w + j:4 * w + j + 1],
                                scale=SCALE).then_inc(sSC.sem, 1)
                    em.add("scalar", ssg)
                    sgb = (sSC.n + 1, sSC.n + 2, sSC.n + 3, sSC.n + 4)
                    sSC.n += 4
                    cur_sg = sSC.n

                    def pom(t, w=w, nqc=nqc, sl2=sl2, sgb=sgb,
                            wf1=prev["on"], wf2=prev["den"]):
                        if wf1:
                            t.wait_ge(sSC.sem, wf1)
                        if wf2:
                            t.wait_ge(sVE.sem, wf2)
                        t.wait_ge(sSC.sem, sgb[1])
                        for qc in range(nqc):
                            for j in range(4):
                                if j == 2 and qc == 0:
                                    t.wait_ge(sSC.sem, sgb[3])
                                ins = t.matmul(
                                    ps_o[qc // 2][:, 130 * (qc % 2):
                                                  130 * (qc % 2) + 129],
                                    sgm[:, sl2, 512 * j + 128 * qc:
                                        512 * j + 128 * (qc + 1)],
                                    vaug[:, 130 * (2 * w + j):
                                         130 * (2 * w + j) + 129],
                                    start=(j == 0), stop=(j == 3))
                                if qc == nqc - 1 and j == 3:
                                    ins.then_inc(sPE.sem, 1)
                    em.add("tensor", pom)
                    sPE.n += 1
                    cur_om = sPE.n

                    def vden(v, w=w, qw0=qw0, nqc=nqc, sl2=sl2, wp=cur_om,
                             wfree=prev2["on"], base=sVE.n):
                        v.wait_ge(sPE.sem, wp)
                        if wfree:
                            v.wait_ge(sSC.sem, wfree)
                        c0 = qw0 // 128
                        nb = (nqc + 1) // 2
                        k = 0
                        for b in range(nb):
                            nn = min(2, nqc - 2 * b)
                            ap_den = bass.AP(ps_o[b], 128,
                                             [[512, 128], [130, nn]])
                            v.reciprocal(den[:, sl2, 2 * b:2 * b + nn], ap_den
                                         ).then_inc(sVE.sem, 1)
                            k += 1
                            v.wait_ge(sVE.sem, base + k)
                            v.tensor_tensor(
                                rbw[:, sl2, 2 * b:2 * b + nn],
                                den[:, sl2, 2 * b:2 * b + nn],
                                t_bl[:, 4 * w + c0 + 2 * b:
                                     4 * w + c0 + 2 * b + nn],
                                op=ALU.mult).then_inc(sVE.sem, 1)
                            k += 1
                    em.add("vector", vden)
                    sVE.n += 2 * ((nqc + 1) // 2)
                    cur_den = sVE.n

                    def son(s, nqc=nqc, sl2=sl2, wv_=cur_den,
                            wfree=prev2["acc"]):
                        s.wait_ge(sVE.sem, wv_)
                        if wfree:
                            s.wait_ge(sVE.sem, wfree)
                        for qc in range(nqc):
                            ins = s.activation(
                                onr[:, sl2, 128 * qc:128 * (qc + 1)],
                                ps_o[qc // 2][:, 130 * (qc % 2):
                                              130 * (qc % 2) + 128],
                                AF.Identity,
                                scale=rbw[:, sl2, qc:qc + 1])
                            if qc == nqc - 1:
                                ins.then_inc(sSC.sem, 1)
                    em.add("scalar", son)
                    sSC.n += 1
                    cur_on = sSC.n

                    def vacc(v, h=h, w=w, qw0=qw0, nqc=nqc, sl2=sl2,
                             ws=cur_on, wprev=prev["acc"]):
                        v.wait_ge(sSC.sem, ws)
                        if wprev:
                            v.wait_ge(sVE.sem, wprev)
                        c0 = qw0 // 128
                        parts = []
                        for role in ("copy", "add"):
                            cls = [qc for qc in range(nqc)
                                   if ((c0 + qc) >= 2) == (role == "copy")]
                            if cls:
                                parts.append((role, cls[0], len(cls)))
                        for k, (role, qca, nn) in enumerate(parts):
                            ac0 = 2 * (w - 1) + c0 + qca
                            ap_att = bass.AP(att, D * ac0 + 128 * h,
                                             [[8 * D, 128], [D, nn], [1, 128]])
                            ap_on = bass.AP(onr, 512 * sl2 + 128 * qca,
                                            [[2 * 512, 128], [128, nn],
                                             [1, 128]])
                            if role == "copy":
                                ins = v.tensor_copy(ap_att, ap_on)
                            else:
                                ins = v.tensor_tensor(ap_att, ap_att, ap_on,
                                                      op=ALU.add)
                            if k == len(parts) - 1:
                                ins.then_inc(sVE.sem, 1)
                    em.add("vector", vacc)
                    sVE.n += 1
                    prev2 = prev
                    prev = {"sg": cur_sg, "sgb": sgb, "om": cur_om,
                            "den": cur_den, "on": cur_on, "acc": sVE.n,
                            "sc": cur_sc}
                    if h + 1 < HEADS and w < 4:
                        wqtr(w)
                        if w == 3:
                            sc_wqt = sSC.n
            em.flush()
            pst_free = [[], []]

        kv_scope.close()
        xln_scope.close()

        attf_scope = ExitStack()
        attf = attf_scope.enter_context(
            nc.sbuf_tensor("attf", [128, 32 * 1024], BF16))
        for nci in range(32):
            dst = bass.AP(attf, 1024 * nci, [[32 * 1024, 128], [1, 1024]])
            tr8(att, [D * i + 128 * nci for i in range(8)], dst)
        em.flush()
        pst_free = [[], []]
        att_scope.close()

        ofm_scope = ExitStack()
        ofm = ofm_scope.enter_context(
            nc.sbuf_tensor("ofm", [128, 32 * TOK], BF16, side="right"))
        with ExitStack() as ph:
            e = ph.enter_context
            wor = e(nc.sbuf_tensor("wor", [128, 2, 2048], F32))
            wob = e(nc.sbuf_tensor("wob", [128, D], BF16))
            wot = e(nc.sbuf_tensor("wot", [128, D], BF16))

            v_cast = 0
            sc_wot = 0
            p_omm = 0
            sc_oev = 0
            for m in range(32):
                toks = []
                for half in range(2):
                    dwaits = [(sVE.sem, v_cast)] if (m >= 1 and half == 0) \
                        else []
                    toks.append(LW.issue(
                        em, "sync",
                        lambda s, m=m, half=half: s.dma_start(
                            wor[:, half, :],
                            wo[128 * m:128 * (m + 1),
                               2048 * half:2048 * (half + 1)]),
                        waits=dwaits))

                def vcast(v, w0=toks[0], w1=toks[1], wf=sc_wot):
                    v.wait_ge(*w0)
                    v.wait_ge(*w1)
                    if wf:
                        v.wait_ge(sSC.sem, wf)
                    v.tensor_copy(wob[:, 0:2048], wor[:, 0, :])
                    v.tensor_copy(wob[:, 2048:4096], wor[:, 1, :]
                                  ).then_inc(sVE.sem, 1)
                em.add("vector", vcast)
                sVE.n += 1
                v_cast = sVE.n

                def gate_t(t, w=v_cast, wf=p_omm):
                    t.wait_ge(sVE.sem, w)
                    if wf:
                        t.wait_ge(sPE.sem, wf)
                em.add("tensor", gate_t)
                for b in range(4):
                    d0 = 8 * b
                    tr8(wob, [128 * (d0 + j) for j in range(8)],
                        wot[:, 1024 * b:1024 * (b + 1)])
                sc_wot = sSC.n

                def omm(t, w=sc_wot, wf=sc_oev):
                    t.wait_ge(sSC.sem, w)
                    if wf:
                        t.wait_ge(sSC.sem, wf)
                    for th in range(2):
                        for n in range(32):
                            ins = t.matmul(ps_a[th][:, :],
                                           wot[:, 128 * n:
                                               128 * (n + 1)],
                                           attf[:, 1024 * n + 512 * th:
                                                1024 * n + 512 * (th + 1)],
                                           start=(n == 0), stop=(n == 31))
                            if n == 31:
                                ins.then_inc(sPE.sem, 1)
                em.add("tensor", omm)
                sPE.n += 2
                p_omm = sPE.n

                def oev(s, m=m, w=p_omm):
                    s.wait_ge(sPE.sem, w - 1)
                    s.copy(ofm[:, TOK * m + 0:TOK * m + 512], ps_a[0][:, :])
                    s.wait_ge(sPE.sem, w)
                    s.copy(ofm[:, TOK * m + 512:TOK * m + 1024],
                           ps_a[1][:, :]).then_inc(sSC.sem, 1)
                em.add("scalar", oev)
                sSC.n += 1
                sc_oev = sSC.n
            em.flush()
            pst_free = [[], []]
        attf_scope.close()

        with ExitStack() as ph:
            e = ph.enter_context
            xrb = e(nc.sbuf_tensor("xrb", [128, 2, D], F32))
            r2s = e(nc.sbuf_tensor("r2s", [128, 2, D], F32))

            v_add = [0] * 8
            d_out = [(None, 0)] * 8
            half_free = [0, 0]
            for tc in range(8):
                dwaits = []
                if tc >= 2:
                    dwaits.append((sVE.sem, v_add[tc - 2]))
                tok = LW.issue(em, "sync",
                               lambda s, tc=tc: s.dma_start(
                                   xrb[:, tc % 2, :],
                                   xs[HALO + 128 * tc:
                                      HALO + 128 * (tc + 1), :]),
                               waits=dwaits)

                for b in range(4):
                    half = b % 2

                    def ptr(t, tc=tc, b=b, half=half, w=half_free[half]):
                        if w:
                            t.wait_ge(sVE.sem, w)
                        for m8 in range(8):
                            m = 8 * b + m8
                            ins = t.transpose(
                                pst(half)[:, 128 * m8:128 * (m8 + 1)],
                                ofm[:, TOK * m + 128 * tc:
                                    TOK * m + 128 * (tc + 1)],
                                t_id[:, :])
                            if m8 == 7:
                                ins.then_inc(sPE.sem, 1)
                    em.add("tensor", ptr)
                    sPE.n += 1
                    p_tr = sPE.n

                    def vadd(v, tc=tc, b=b, half=half, wp=p_tr, wd=tok,
                             wo_=(d_out[tc - 2] if tc >= 2 else (None, 0))):
                        v.wait_ge(sPE.sem, wp)
                        if b == 0:
                            v.wait_ge(*wd)
                            if wo_[1]:
                                v.wait_ge(*wo_)
                        v.tensor_tensor(r2s[:, tc % 2, 1024 * b:1024 * (b + 1)],
                                        xrb[:, tc % 2, 1024 * b:1024 * (b + 1)],
                                        pst(half), op=ALU.add
                                        ).then_inc(sVE.sem, 1)
                    em.add("vector", vadd)
                    sVE.n += 1
                    half_free[half] = sVE.n
                v_add[tc] = sVE.n

                d_out[tc] = LO.issue(em, "scalar",
                                     lambda s, tc=tc: s.dma_start(
                                         r2[128 * tc:128 * (tc + 1), :],
                                         r2s[:, tc % 2, :]),
                                     waits=[(sVE.sem, v_add[tc])])

            em.add("scalar", LO.wait_all_fn())
            em.flush()
            pst_free = [[], []]
        ofm_scope.close()

        if upto == "attn":
            return nc

        with ExitStack() as ph:
            e = ph.enter_context
            xl2 = e(nc.sbuf_tensor("xl2", [128, 32 * 512], BF16))
            hbuf = e(nc.sbuf_tensor("hbuf", [128, NG * 512], BF16))
            ring = e(nc.sbuf_tensor("ring", [128, 10, 512], I32))
            deq = e(nc.sbuf_tensor("deq", [128, 8, 512], BF16))
            ssl = e(nc.sbuf_tensor("ssl", [128, 3, 512], F32))
            ssb = e(nc.sbuf_tensor("ssb", [128, 3, 512], BF16))
            silu = e(nc.sbuf_tensor("silu", [128, 4, 512], BF16))
            r2c = e(nc.sbuf_tensor("r2c", [128, D], F32))
            xlt2 = e(nc.sbuf_tensor("xlt2", [128, D], BF16))
            sts2 = e(nc.sbuf_tensor("sts2", [128, 48], F32))
            mv2 = e(nc.sbuf_tensor("mv2", [128, 2], F32))
            rstd2 = e(nc.sbuf_tensor("rstd2", [128, 1], F32))
            nmb2 = e(nc.sbuf_tensor("nmb2", [128, 1], F32))
            r2sl = e(nc.sbuf_tensor("r2sl", [128, 4, 512], F32))
            outp = e(nc.sbuf_tensor("outp", [128, 2, 512], F32))

            raw_free = []
            n_raw = 0
            ring_guard = 0
            epoch_start = 0
            deq_free = {}
            deq_wtok = {}
            wtoks = []
            ld_cur = 0
            n_deq = 0
            pe_gu_end = 0
            pe_dn_end = 0
            v_hm_last = 0
            sc_si_last = 0
            v_out_all = 0
            d_outp = [(None, 0), (None, 0)]
            r2sl_free = [0] * 4
            ssb_free = [0, 0, 0]
            ssl_free = [0, 0, 0]
            bcg = [[], [], [], []]

            def bcbank(i, REG=512):
                if i == 0:
                    return psf(0)[:, 0:REG]
                if i == 1:
                    return psf(1)[:, 0:REG]
                return ps_o[i - 2][:, 0:REG]
            NREG = 22
            regw = [min(512, II - 512 * r) for r in range(NREG)]

            def deq_guard(k):
                ws = []
                if k >= 0:
                    v = deq_free.get(k, 0)
                    if v:
                        ws.append((sPE.sem, v))
                    t = deq_wtok.get(k, (None, 0))
                    if t[1]:
                        ws.append(t)
                return ws

            for hf in range(2):
                lf = (0, 0)
                for tc in range(4):
                    dwaits = [(sVE.sem, lf[0]), (sSC.sem, lf[1])] \
                        if lf[1] else []
                    tok = LW.issue(em, "sync",
                                   lambda s, hf=hf, tc=tc: s.dma_start(
                                       r2c[:, :],
                                       r2[512 * hf + 128 * tc:
                                          512 * hf + 128 * (tc + 1), :]),
                                   waits=dwaits)
                    waits = [tok]
                    if lf[1]:
                        waits.append((sSC.sem, lf[1]))
                    lf = ln_chunk(lambda: r2c[:, :], sts2, mv2, rstd2,
                                  nmb2, xlt2, waits)

                    if hf == 1 and tc == 0:
                        em.add("scalar",
                               lambda s, w=pe_gu_end: s.wait_ge(sPE.sem, w))
                    for b in range(4):
                        d0 = 8 * b
                        ap = bass.AP(xl2, 512 * d0 + 128 * tc,
                                     [[32 * 512, 128], [512, 8], [1, 128]])
                        tr8(xlt2, [128 * (d0 + j) for j in range(8)], ap,
                            extra_pe_wait=lf[1])
                ln2_done = sSC.n
                bcg[0] = list(pst_free[0])
                bcg[1] = list(pst_free[1])

                for r in range(NREG):
                    REG = regw[r]
                    ngc = REG // 128

                    for gi, (gmat, smat) in enumerate(
                            ((gq, gs), (uq, us))):
                        pend = []

                        def emit_pmm(item):
                            k, fn = item
                            em.add("tensor", fn)
                            sPE.n += 1
                            deq_free[k] = sPE.n

                        def mkrep(s, smat=smat, r=r, gi=gi, REG=REG):
                            ap_in = bass.AP(
                                smat.tensor, 512 * r,
                                [[0, 4], [II, 32], [1, REG]])
                            ap_out = bass.AP(
                                ssl, 3 * 512 * 0 + 512 * gi,
                                [[3 * 512, 128], [1, REG]])
                            return s.dma_start(ap_out, ap_in)
                        stok = LW.issue(
                            em, "sync", mkrep,
                            waits=[(sVE.sem, ssl_free[gi])]
                            if ssl_free[gi] else [])

                        def vcs(v, gi=gi, REG=REG, w=stok,
                                wf=ssb_free[gi]):
                            v.wait_ge(*w)
                            if wf:
                                v.wait_ge(sPE.sem, wf)
                            v.tensor_copy(ssb[:, gi, 0:REG],
                                          ssl[:, gi, 0:REG]
                                          ).then_inc(sVE.sem, 1)
                        em.add("vector", vcs)
                        sVE.n += 1
                        v_sb = sVE.n
                        ssl_free[gi] = v_sb

                        d_raw_l = []
                        base_raw = n_raw

                        def issue_raw(gp, r=r, gmat=gmat, REG=REG):
                            nonlocal n_raw
                            slot = n_raw % 5
                            wfree = (raw_free[n_raw - 5]
                                     if n_raw - 5 >= epoch_start
                                     else ring_guard)
                            d_raw_l.append(LW.issue(
                                em, "sync",
                                lambda s, gp=gp, gmat=gmat, REG=REG,
                                slot=slot, r=r: s.dma_start(
                                    ring[:, 2 * slot:2 * slot + 2,
                                         0:REG],
                                    gmat[256 * gp:256 * (gp + 1),
                                         512 * r:512 * r + REG]
                                    .rearrange("(b p) i -> p b i", b=2)),
                                waits=[(sVE.sem, wfree)] if wfree
                                else []))
                            raw_free.append(0)
                            n_raw += 1

                        for gp in range(5):
                            issue_raw(gp)

                        for g in range(32):
                            if g % 2 == 0 and g >= 2 and g // 2 + 4 < 16:
                                issue_raw(g // 2 + 4)
                            slot = (base_raw + g // 2) % 5
                            bk = g % 4

                            if bk == 0:
                                def pbc4(t, g=g, gi=gi, REG=REG, ws=v_sb,
                                         guards=tuple(tuple(bcg[i])
                                                      for i in range(4))):
                                    t.wait_ge(sVE.sem, ws)
                                    for gd in guards:
                                        for gsem, gval in gd:
                                            t.wait_ge(gsem, gval)
                                    for i in range(4):
                                        t.matmul(
                                            bcbank(i, REG),
                                            t_e32[32 * i:32 * (i + 1),
                                                  128 * (g + i):
                                                  128 * (g + i + 1)],
                                            ssb[32 * i:32 * (i + 1),
                                                gi, 0:REG],
                                            start=True, stop=True,
                                            tile_position=(32 * i, 0)
                                            ).then_inc(sPE.sem, 1)
                                em.add("tensor", pbc4)
                                sPE.n += 4
                                p_bc = sPE.n
                                ssb_free[gi] = p_bc

                            dslot = n_deq % 8
                            gw = deq_guard(n_deq - 8)

                            def vdq(v, g=g, REG=REG, bk=bk,
                                    slot=slot, dslot=dslot, wb=p_bc,
                                    wr=d_raw_l[g // 2], gw=tuple(gw)):
                                v.wait_ge(sPE.sem, wb)
                                v.wait_ge(*wr)
                                for gsem, gval in gw:
                                    v.wait_ge(gsem, gval)
                                v.scalar_tensor_tensor(
                                    deq[:, dslot, 0:REG],
                                    ring[:, 2 * slot + (g % 2), 0:REG],
                                    -8.0, bcbank(bk, REG),
                                    op0=ALU.add, op1=ALU.mult
                                    ).then_inc(sVE.sem, 1)
                            em.add("vector", vdq)
                            sVE.n += 1
                            v_dq = sVE.n
                            bcg[bk] = [(sVE.sem, v_dq)]
                            raw_free[base_raw + g // 2] = v_dq

                            wv_ = v_hm_last if (g == 0 and gi == 0) else 0
                            wsc = sc_si_last if (g == 0 and gi == 1) else 0
                            wln = ln2_done if g == 0 else 0

                            def pmm(t, g=g, ngc=ngc, dslot=dslot,
                                    wd=v_dq, wv_=wv_, wsc=wsc, wln=wln):
                                t.wait_ge(sVE.sem, wd)
                                if wv_:
                                    t.wait_ge(sVE.sem, wv_)
                                if wsc:
                                    t.wait_ge(sSC.sem, wsc)
                                if wln:
                                    t.wait_ge(sSC.sem, wln)
                                for icl in range(ngc):
                                    ins = t.matmul(
                                        ps_a[icl][:, :],
                                        deq[:, dslot,
                                            128 * icl:128 * (icl + 1)],
                                        xl2[:, 512 * g:512 * (g + 1)],
                                        start=(g == 0), stop=(g == 31))
                                    if icl == ngc - 1:
                                        ins.then_inc(sPE.sem, 1)
                            pend.append((n_deq, pmm))
                            n_deq += 1
                            if len(pend) > 3:
                                emit_pmm(pend.pop(0))
                        for item in pend:
                            emit_pmm(item)
                        p_gu = sPE.n

                        if gi == 0:
                            def ssi(s, ngc=ngc, w=p_gu, wf=v_hm_last):
                                s.wait_ge(sPE.sem, w)
                                if wf:
                                    s.wait_ge(sVE.sem, wf)
                                for icl in range(ngc):
                                    ins = s.activation(silu[:, icl, :],
                                                       ps_a[icl][:, :],
                                                       AF.Silu)
                                    if icl == ngc - 1:
                                        ins.then_inc(sSC.sem, 1)
                            em.add("scalar", ssi)
                            sSC.n += 1
                            sc_si_last = sSC.n
                        else:
                            def vhm(v, r=r, ngc=ngc, w=p_gu, ws=sc_si_last,
                                    wdn=pe_dn_end):
                                v.wait_ge(sPE.sem, w)
                                v.wait_ge(sSC.sem, ws)
                                if wdn:
                                    v.wait_ge(sPE.sem, wdn)
                                for icl in range(ngc):
                                    ic = 4 * r + icl
                                    ins = v.tensor_tensor(
                                        hbuf[:, 512 * ic:512 * (ic + 1)],
                                        silu[:, icl, :], ps_a[icl][:, :],
                                        op=ALU.mult)
                                    if icl == ngc - 1:
                                        ins.then_inc(sVE.sem, 1)
                            em.add("vector", vhm)
                            sVE.n += 1
                            v_hm_last = sVE.n
                pe_gu_end = sPE.n
                ring_guard = sVE.n
                epoch_start = n_raw

                for m in range(8):
                    pend = []

                    def emit_pmm(item):
                        k, fn = item
                        em.add("tensor", fn)
                        sPE.n += 1
                        deq_free[k] = sPE.n

                    v_sd = 0
                    for b in range(3):
                        def mkrepd(s, m=m, b=b):
                            ap_in = bass.AP(
                                dsr.tensor, 4096 * b + 512 * m,
                                [[0, 4], [3 * D, 32], [1, 512]])
                            ap_out = bass.AP(
                                ssl, 512 * b,
                                [[3 * 512, 128], [1, 512]])
                            return s.dma_start(ap_out, ap_in)
                        stok = LW.issue(
                            em, "sync", mkrepd,
                            waits=[(sVE.sem, ssl_free[b])]
                            if ssl_free[b] else [])

                        def vcs(v, b=b, w=stok, wf=ssb_free[b]):
                            v.wait_ge(*w)
                            if wf:
                                v.wait_ge(sPE.sem, wf)
                            v.tensor_copy(ssb[:, b, :], ssl[:, b, :]
                                          ).then_inc(sVE.sem, 1)
                        em.add("vector", vcs)
                        sVE.n += 1
                        v_sd = sVE.n
                        ssl_free[b] = v_sd

                    d_raw_l = []
                    base_raw = n_raw
                    gbat = [(2 * i, min(2, NG - 2 * i))
                            for i in range(43)]

                    def issue_raw_dn(bi, m=m):
                        nonlocal n_raw
                        g0, nb = gbat[bi]
                        slot = n_raw % 5
                        wfree = (raw_free[n_raw - 5]
                                 if n_raw - 5 >= epoch_start
                                 else ring_guard)
                        d_raw_l.append(LW.issue(
                            em, "sync",
                            lambda s, m=m, g0=g0, nb=nb, slot=slot:
                            s.dma_start(
                                ring[:, 2 * slot:2 * slot + nb, :],
                                dq[128 * g0:128 * (g0 + nb),
                                   512 * m:512 * (m + 1)]
                                .rearrange("(b p) i -> p b i", b=nb)),
                            waits=[(sVE.sem, wfree)] if wfree else []))
                        raw_free.append(0)
                        n_raw += 1

                    for bi in range(5):
                        issue_raw_dn(bi)

                    for g in range(NG):
                        if g % 2 == 0 and g >= 2 and g // 2 + 4 < 43:
                            issue_raw_dn(g // 2 + 4)
                        slot = (base_raw + g // 2) % 5
                        bk = g % 4

                        if bk == 0:
                            nb4 = min(4, NG - g)

                            def pbc4(t, g=g, nb4=nb4, ws=v_sd,
                                     guards=tuple(tuple(bcg[i])
                                                  for i in range(4))):
                                t.wait_ge(sVE.sem, ws)
                                for gd in guards:
                                    for gsem, gval in gd:
                                        t.wait_ge(gsem, gval)
                                for i in range(nb4):
                                    gg = (g + i) % 32
                                    t.matmul(
                                        bcbank(i),
                                        t_e32[32 * i:32 * (i + 1),
                                              128 * gg:128 * (gg + 1)],
                                        ssb[32 * i:32 * (i + 1),
                                            (g + i) // 32, :],
                                        start=True, stop=True,
                                        tile_position=(32 * i, 0)
                                        ).then_inc(sPE.sem, 1)
                            em.add("tensor", pbc4)
                            sPE.n += nb4
                            p_bc = sPE.n
                            ssb_free[g // 32] = p_bc

                        dslot = n_deq % 8
                        gw = deq_guard(n_deq - 8)

                        def vdq(v, g=g, bk=bk, slot=slot,
                                dslot=dslot, wb=p_bc,
                                wr=d_raw_l[g // 2], gw=tuple(gw)):
                            v.wait_ge(sPE.sem, wb)
                            v.wait_ge(*wr)
                            for gsem, gval in gw:
                                v.wait_ge(gsem, gval)
                            v.scalar_tensor_tensor(
                                deq[:, dslot, :],
                                ring[:, 2 * slot + (g % 2), :],
                                -8.0, bcbank(bk),
                                op0=ALU.add, op1=ALU.mult
                                ).then_inc(sVE.sem, 1)
                        em.add("vector", vdq)
                        sVE.n += 1
                        v_dq = sVE.n
                        bcg[bk] = [(sVE.sem, v_dq)]
                        raw_free[base_raw + g // 2] = v_dq

                        wv_ = v_hm_last if g == 0 else 0
                        wob = tuple(r2sl_free) if g == 0 else ()

                        def pmm(t, g=g, dslot=dslot, wd=v_dq, wv_=wv_,
                                wob=wob):
                            t.wait_ge(sVE.sem, wd)
                            if wv_:
                                t.wait_ge(sVE.sem, wv_)
                            for tc in range(4):
                                if wob and wob[tc]:
                                    t.wait_ge(sVE.sem, wob[tc])
                                ins = t.matmul(
                                    ps_a[tc][:, :],
                                    hbuf[:, 512 * g + 128 * tc:
                                         512 * g + 128 * (tc + 1)],
                                    deq[:, dslot, :],
                                    start=(g == 0), stop=(g == NG - 1))
                                if tc == 3:
                                    ins.then_inc(sPE.sem, 1)
                        pend.append((n_deq, pmm))
                        n_deq += 1
                        if len(pend) > 3:
                            emit_pmm(pend.pop(0))
                    for item in pend:
                        emit_pmm(item)
                    p_dn = sPE.n

                    for tc in range(4):
                        rtok = LW.issue(
                            em, "sync",
                            lambda s, hf=hf, m=m, tc=tc: s.dma_start(
                                r2sl[:, tc, :],
                                r2[512 * hf + 128 * tc:
                                   512 * hf + 128 * (tc + 1),
                                   512 * m:512 * (m + 1)]),
                            waits=[(sVE.sem, r2sl_free[tc])]
                            if r2sl_free[tc] else [])

                        def vout(v, tc=tc, wp=p_dn, wd=rtok,
                                 wo_=d_outp[tc % 2]):
                            v.wait_ge(sPE.sem, wp)
                            v.wait_ge(*wd)
                            if wo_[1]:
                                v.wait_ge(*wo_)
                            v.tensor_tensor(outp[:, tc % 2, :],
                                            ps_a[tc][:, :], r2sl[:, tc, :],
                                            op=ALU.add).then_inc(sVE.sem, 1)
                        em.add("vector", vout)
                        sVE.n += 1
                        r2sl_free[tc] = sVE.n
                        v_o = sVE.n

                        d_outp[tc % 2] = LO.issue(
                            em, "scalar",
                            lambda s, hf=hf, m=m, tc=tc: s.dma_start(
                                out[512 * hf + 128 * tc:
                                    512 * hf + 128 * (tc + 1),
                                    512 * m:512 * (m + 1)],
                                outp[:, tc % 2, :]),
                            waits=[(sVE.sem, v_o)])
                    v_out_all = sVE.n
                pe_dn_end = sPE.n
                ring_guard = sVE.n
                epoch_start = n_raw
                pst_free[0] = list(bcg[0])
                pst_free[1] = list(bcg[1])

            em.add("scalar", LO.wait_all_fn())
            em.flush()

    return nc



_ROPE_BASE = 10000.0


def _host_constants(core):
    p0 = TOK * core
    t = np.arange(p0 - HALO, p0 - HALO + XT, dtype=np.float32)
    inv = 1.0 / _ROPE_BASE ** (np.arange(0, DH, 2, dtype=np.float32) / DH)
    f = np.outer(t, inv)
    cos2 = np.cos(f).astype(np.float32)
    sin2 = np.sin(f).astype(np.float32)
    cos = np.empty((XT, DH), np.float32)
    sin = np.empty((XT, DH), np.float32)
    cos[:, 0::2] = cos2
    cos[:, 1::2] = cos2
    sin[:, 0::2] = sin2
    sin[:, 1::2] = sin2
    c_cos = np.ascontiguousarray(cos.T)
    c_sin = np.ascontiguousarray(sin.T)

    rot = np.zeros((DH, DH), np.float32)
    for dp in range(64):
        rot[dp + 64, dp] = -1.0
    for dp in range(64, 128):
        rot[dp - 64, dp] = 1.0
    c_rt = rot.astype(BFDT)

    ident = np.eye(128, dtype=BFDT)

    e32 = np.zeros((32, 4096), np.float32)
    for g in range(32):
        e32[g, 128 * g:128 * (g + 1)] = 1.0
    c_e32 = np.ascontiguousarray(np.tile(e32, (4, 1))).astype(BFDT)

    km = np.zeros((128, 20), np.float32)
    for w in range(NW):
        for j in range(4):
            tau = 256 * w + 128 * j + np.arange(128)
            glob = (p0 - HALO) + tau
            km[:, 4 * w + j] = np.where((glob >= 0) & (glob < S), 0.0, NEG)

    bl = np.zeros((128, 20), np.float32)
    for w in range(NW):
        gw = p0 + 256 * (w - 1)
        for c in range(4):
            jq = 128 * c + np.arange(128)
            if c < 2:
                wt = jq / 255.0
                if gw == 0:
                    wt = np.ones(128)
            else:
                wt = 1.0 - (jq - 256) / 255.0
                if gw < 0:
                    wt = np.zeros(128)
            bl[:, 4 * w + c] = wt
    return {"c_cos": c_cos, "c_sin": c_sin, "c_rt": c_rt, "c_id": ident,
            "c_e32": c_e32, "c_km": km.astype(np.float32),
            "c_bl": bl.astype(np.float32)}


def _prep_inputs(L0_attn_x, L0_attn_Wq, L0_attn_Wk, L0_attn_Wv, L0_attn_Wo,
                 L0_ffn_gate_q, L0_ffn_gate_s, L0_ffn_up_q, L0_ffn_up_s,
                 L0_ffn_down_q, L0_ffn_down_s):
    x = np.ascontiguousarray(np.asarray(L0_attn_x, dtype=np.float32)
                             .reshape(S, D))
    ds = np.asarray(L0_ffn_down_s, dtype=np.float32)
    dsr = np.zeros((32, 3, D), np.float32)
    for g in range(ds.shape[0]):
        dsr[g % 32, g // 32, :] = ds[g]
    dsr = np.ascontiguousarray(dsr.reshape(32, 3 * D))

    shared = {
        "wq": np.ascontiguousarray(np.asarray(L0_attn_Wq, np.float32)),
        "wk": np.ascontiguousarray(np.asarray(L0_attn_Wk, np.float32)),
        "wv": np.ascontiguousarray(np.asarray(L0_attn_Wv, np.float32)),
        "wo": np.ascontiguousarray(np.asarray(L0_attn_Wo, np.float32)),
        "gq": np.ascontiguousarray(np.asarray(L0_ffn_gate_q, np.int32)),
        "uq": np.ascontiguousarray(np.asarray(L0_ffn_up_q, np.int32)),
        "dq": np.ascontiguousarray(np.asarray(L0_ffn_down_q, np.int32)),
        "gs": np.ascontiguousarray(np.asarray(L0_ffn_gate_s, np.float32)),
        "us": np.ascontiguousarray(np.asarray(L0_ffn_up_s, np.float32)),
        "dsr": dsr,
    }
    in_maps = []
    for core in range(NCORES):
        p0 = TOK * core
        xsl = np.zeros((XT, D), np.float32)
        lo = max(0, p0 - HALO)
        hi = min(S, p0 + TOK + HALO)
        xsl[lo - (p0 - HALO):hi - (p0 - HALO)] = x[lo:hi]
        m = dict(shared)
        m["xs"] = xsl
        m.update(_host_constants(core))
        in_maps.append(m)
    return in_maps


_CACHE = {}


def kernel(**inputs):
    in_maps = _prep_inputs(**inputs)
    if "nc" not in _CACHE:
        _CACHE["nc"] = build_program("full")
    res = run_bass_kernel_spmd(_CACHE["nc"], in_maps,
                               core_ids=list(range(NCORES)))
    y = np.empty((1, S, D), np.float32)
    for core in range(NCORES):
        y[0, TOK * core:TOK * (core + 1), :] = res.results[core]["out"]
    return y
